# version 05
# baseline (speedup 1.0000x reference)
import numpy as np
import ml_dtypes
import concourse.bass as bass
import concourse.mybir as mybir
from concourse.bass_utils import run_bass_kernel_spmd

F32 = mybir.dt.float32
BF16 = mybir.dt.bfloat16
I32 = mybir.dt.int32
AF = mybir.ActivationFunctionType
ALU = mybir.AluOpType
AX = mybir.AxisListType

D = 2048
DK = D // 128
NSEQ = 2
H = 16
DFF = 8192
KO = DFF // 128
HID = 5632
HC = HID // 128
EPS_RMS = 1.1920929e-07
EPS_LN = 1e-5
NEG_BIG = -1.0e30
ROPE_BASE = 10000.0
ISQ = 1.0 / float(np.sqrt(np.float32(128.0)))
NCORES = 8


class MK:
    def __init__(self, T_seq=2048, n_sub=4):
        assert T_seq % 512 == 0
        self.T = T_seq
        self.NT = T_seq // 128
        self.TC = T_seq // 512
        self.n_sub = n_sub
        self.nc = bass.Bass("TRN2", target_bir_lowering=False, debug=False,
                            detect_race_conditions=False)
        self.ctr = {}

    def inc(self, inst, name, n=1):
        inst.then_inc(self.sems[name], n)
        self.ctr[name] = self.ctr.get(name, 0) + n

    def dinc(self, inst, name):
        self.inc(inst, name, 16)

    def cnt(self, name):
        return self.ctr.get(name, 0)

    def base(self, *names):
        return {n: self.cnt(n) for n in names}

    def fence(self, eng, inst):
        name = {"DVE": "fzv", "Activation": "fzs", "Pool": "fzp",
                "PE": "fzt", "SP": "fzq"}[str(eng.engine).split(".")[-1]]
        self.inc(inst, name)
        eng.wait_ge(self.sems[name], self.cnt(name))

    def build(self):
        nc = self.nc
        T, TT = self.T, NSEQ * self.T

        def din(name, shape, dt=F32):
            setattr(self, name, nc.dram_tensor(name, shape, dt, kind="ExternalInput").ap())

        dbg = getattr(self, "dbg_outs", ())

        def scr(name, shape, dt):
            kind = "ExternalOutput" if name in dbg else "Internal"
            setattr(self, name, nc.dram_tensor(name, shape, dt, kind=kind).ap())

        din("x0", [TT, D])
        din("wqkvT", [D, 3 * D])
        din("wswT", [D, 2 * DFF])
        din("woutT", [DFF, D])
        din("wqT", [D, D]); din("wkT", [D, D]); din("wvT", [D, D]); din("woT", [D, D])
        din("wsA", [D, 64]); din("wsB", [D, 64])
        din("wgT_i", [D, HID], I32); din("wuT_i", [D, HID], I32); din("wdT_i", [HID, D], I32)
        din("s_gate_t", [128, HC]); din("s_up_t", [128, HC]); din("s_down_pad", [2, D])
        din("bf_col", [16, 1])
        din("ident_bf", [128, 128], BF16)
        din("perm_swap", [128, 128], BF16)
        din("ones2", [2, 128])
        din("onehots", [16, 16 * 128])
        din("ropeC1", [128, T], BF16); din("ropeC2", [128, T], BF16)
        din("causal_m", [128, 128])
        self.y_out = nc.dram_tensor("y", [TT, D], F32, kind="ExternalOutput").ap()

        scr("xA", [TT, D], F32); scr("xB", [TT, D], F32); scr("xC", [TT, D], F32)
        scr("q_a", [TT, D], F32); scr("k_a", [TT, D], F32); scr("v_a", [TT, D], F32)
        scr("qrs_ft", [NSEQ, DK, 128, T], BF16)
        scr("krs_ft", [NSEQ, DK, 128, T], BF16)
        scr("v_ft", [NSEQ, DK, 128, T], BF16)
        scr("h_a", [NSEQ, KO, 128, T], BF16)
        scr("wg_bf", [HC, DK, 128, 128], BF16)
        scr("wu_bf", [HC, DK, 128, 128], BF16)
        scr("wd_bf", [HC, 128, D], BF16)
        scr("h_f", [NSEQ, 128, HC, T], BF16)
        scr("q_raw", [NSEQ, H, 128, T], BF16)
        scr("k_raw", [NSEQ, H, 128, T], BF16)
        scr("v_raw", [NSEQ, H, 128, T], BF16)
        scr("o_ft", [NSEQ, H, 128, T], BF16)
        if getattr(self, "dbg_y", False):
            scr("dbgy", [NSEQ, DK, 128, T], BF16)
        if getattr(self, "dbg_rt", False):
            scr("dbgrt", [1, DK, 128, T], BF16)
        if getattr(self, "dbg_flash", False):
            scr("dbgP", [128, 2048], BF16)
            scr("dbgS", [128, 2048], F32)
            scr("dbgMLB", [128, 8], F32)
            scr("dbgQK", [2, 128, T], BF16)
            scr("dbgB", [128, 2048], F32)

        from contextlib import ExitStack
        with ExitStack() as ctx:
            e = ctx.enter_context
            self.big = e(nc.sbuf_tensor("big", [128, 32768], BF16))
            self.bfa = e(nc.sbuf_tensor("bfa", [128, 32768], BF16))
            self.fa = e(nc.sbuf_tensor("fa", [128, 14336], F32))
            self.st1 = e(nc.sbuf_tensor("st1", [128, 304], F32))
            self.bnb = e(nc.sbuf_tensor("bnb", [128, 24], F32))
            self.id_s = e(nc.sbuf_tensor("id_s", [128, 128], BF16))
            self.pswap_s = e(nc.sbuf_tensor("pswap_s", [128, 128], BF16))
            self.ones2_s = e(nc.sbuf_tensor("ones2_s", [2, 128], F32))
            self.oneh_s = e(nc.sbuf_tensor("oneh_s", [16, 16 * 128], F32))
            self.rc1_s = e(nc.sbuf_tensor("rc1_s", [128, T], BF16))
            self.rc2_s = e(nc.sbuf_tensor("rc2_s", [128, T], BF16))
            self.cm_s = e(nc.sbuf_tensor("cm_s", [128, 128], F32))
            self.sg_s = e(nc.sbuf_tensor("sg_s", [128, HC], F32))
            self.su_s = e(nc.sbuf_tensor("su_s", [128, HC], F32))
            self.bf_s = e(nc.sbuf_tensor("bf_s", [16, 1], F32))
            self.wsA_s = e(nc.sbuf_tensor("wsA_s", [128, DK * 64], BF16))
            self.wsB_s = e(nc.sbuf_tensor("wsB_s", [128, DK * 64], BF16))
            self.psf = [e(nc.psum_tensor(f"psf{i}", [128, 512], F32)) for i in range(8)]
            names = ["ld", "wt", "mm", "m2", "ev", "e2", "dv", "d2", "st", "s2", "sg",
                     "fzv", "fzs", "fzp", "fzt", "fzq"]
            self.sems = {n: e(nc.semaphore(n)) for n in names}
            self.emit()
        return nc

    def pbf(self, i):
        return self.psf[i][:].bitcast(BF16)

    def emit(self):
        self.emit_consts()
        if self.n_sub >= 2:
            self.emit_dequant()
        xo = self.xA if self.n_sub > 1 else self.y_out
        self.emit_attn_a(self.x0, xo)
        if self.n_sub == 1:
            return
        xo = self.xB if self.n_sub > 2 else self.y_out
        self.emit_ffn(self.xA, xo)
        if self.n_sub == 2:
            return
        xo = self.xC if self.n_sub > 3 else self.y_out
        self.emit_attn_b(self.xB, xo)
        if self.n_sub == 3:
            return
        self.emit_ffn(self.xC, self.y_out)

    def emit_consts(self):
        nc = self.nc
        with nc.Block() as b:
            B = self.base("ld")

            @b.sync
            def _(sy):
                for dst, src in [
                    (self.id_s[:], self.ident_bf), (self.pswap_s[:], self.perm_swap),
                    (self.ones2_s[:], self.ones2), (self.oneh_s[:], self.onehots),
                    (self.rc1_s[:], self.ropeC1), (self.rc2_s[:], self.ropeC2),
                    (self.cm_s[:], self.causal_m), (self.sg_s[:], self.s_gate_t),
                    (self.su_s[:], self.s_up_t), (self.bf_s[:], self.bf_col),
                ]:
                    self.dinc(sy.dma_start(out=dst, in_=src), "ld")

            @b.gpsimd
            def _(gp):
                self.dinc(gp.dma_start(
                    out=self.wsA_s[:].rearrange("p (k q) -> p k q", q=64),
                    in_=self.wsA.rearrange("(k p) q -> p k q", p=128)), "wt")
                self.dinc(gp.dma_start(
                    out=self.wsB_s[:].rearrange("p (k q) -> p k q", q=64),
                    in_=self.wsB.rearrange("(k p) q -> p k q", p=128)), "wt")
                gp.wait_ge(self.sems["ld"], self.cnt("ld"))

            @b.vector
            def _(v):
                v.memset(self.st1[:, 11:12], 0.0)
                v.wait_ge(self.sems["wt"], self.cnt("wt"))

    def emit_dequant(self):
        nc = self.nc
        fa, big, bfa = self.fa, self.big, self.bfa
        SDB, SDP = 11264, 0
        with nc.Block() as b:
            B = self.base("ld", "mm", "ev")

            @b.sync
            def _(sy):
                self.dinc(sy.dma_start(out=fa[0:2, SDP:SDP + D], in_=self.s_down_pad), "ld")

            @b.tensor
            def _(t):
                t.wait_ge(self.sems["ld"], B["ld"] + 16)
                for c in range(4):
                    self.inc(t.matmul(self.psf[c][:], self.ones2_s[:],
                                      fa[0:2, SDP + c * 512:SDP + (c + 1) * 512],
                                      start=True, stop=True), "mm")

            @b.scalar
            def _(s):
                for c in range(4):
                    s.wait_ge(self.sems["mm"], B["mm"] + c + 1)
                    self.inc(s.activation(fa[:, SDB + c * 512:SDB + (c + 1) * 512],
                                          self.psf[c][:], AF.Copy), "ev")

        with nc.Block() as b:
            B = self.base("ld", "ev", "st")

            @b.sync
            def _(sy):
                for i in range(2 * DK):
                    if i >= 2:
                        sy.wait_ge(self.sems["ev"], B["ev"] + i - 1)
                    src = self.wgT_i if i < DK else self.wuT_i
                    k = i % DK
                    self.dinc(sy.dma_start(
                        out=fa[:, (i % 2) * HID:(i % 2) * HID + HID].bitcast(I32),
                        in_=src[k * 128:(k + 1) * 128, :]), "ld")

            @b.scalar
            def _(s):
                for i in range(2 * DK):
                    dst = self.wg_bf if i < DK else self.wu_bf
                    k = i % DK
                    s.wait_ge(self.sems["ld"], B["ld"] + (i + 1) * 16)
                    if i >= 2:
                        s.wait_ge(self.sems["st"], B["st"] + (i - 1) * 16)
                    ob = big[:, (i % 2) * HID:(i % 2) * HID + HID]
                    self.inc(s.activation(ob, fa[:, (i % 2) * HID:(i % 2) * HID + HID]
                                          .bitcast(I32), AF.Copy, bias=-128.0), "ev")
                    s.wait_ge(self.sems["ev"], self.cnt("ev"))
                    self.dinc(s.dma_start(
                        out=dst[:, k, :, :].rearrange("h p q -> p h q"),
                        in_=ob.rearrange("p (h q) -> p h q", q=128)), "st")

        with nc.Block() as b:
            B = self.base("ld", "ev", "dv", "sg")

            @b.sync
            def _(sy):
                for k in range(HC):
                    if k >= 2:
                        sy.wait_ge(self.sems["ev"], B["ev"] + k - 1)
                    self.dinc(sy.dma_start(
                        out=fa[:, (k % 2) * D:(k % 2) * D + D].bitcast(I32),
                        in_=self.wdT_i[k * 128:(k + 1) * 128, :]), "ld")

            @b.scalar
            def _(s):
                for k in range(HC):
                    s.wait_ge(self.sems["ld"], B["ld"] + (k + 1) * 16)
                    if k >= 2:
                        s.wait_ge(self.sems["dv"], B["dv"] + k - 1)
                    self.inc(s.activation(fa[:, 4096 + (k % 2) * D:4096 + (k % 2) * D + D],
                                          fa[:, (k % 2) * D:(k % 2) * D + D].bitcast(I32),
                                          AF.Copy, bias=-128.0), "ev")

            @b.vector
            def _(v):
                for k in range(HC):
                    v.wait_ge(self.sems["ev"], B["ev"] + k + 1)
                    if k >= 2:
                        v.wait_ge(self.sems["sg"], B["sg"] + (k - 1) * 16)
                    self.fence(v, v.tensor_mul(bfa[:, (k % 2) * D:(k % 2) * D + D],
                                               fa[:, 4096 + (k % 2) * D:4096 + (k % 2) * D + D],
                                               fa[:, SDB:SDB + D]))
                    self.inc(v.tensor_copy(self.st1[:, 11:12], self.st1[:, 11:12]), "dv")

            @b.gpsimd
            def _(gp):
                for k in range(HC):
                    gp.wait_ge(self.sems["dv"], B["dv"] + k + 1)
                    self.dinc(gp.dma_start(out=self.wd_bf[k, :, :],
                                           in_=bfa[:, (k % 2) * D:(k % 2) * D + D]), "sg")

    def emit_ln(self, x_in, seq, rms_fold, save_rstats):
        nc = self.nc
        NT, T = self.NT, self.T
        fa, st1, bfa = self.fa, self.st1, self.bfa
        with nc.Block() as b:
            B = self.base("ld", "ev", "dv", "d2", "mm", "s2")

            @b.sync
            def _(sy):
                for ti in range(NT):
                    if ti >= 2:
                        sy.wait_ge(self.sems["ev"], B["ev"] + 2 * (ti - 2) + 2)
                    self.dinc(sy.dma_start(
                        out=fa[:, (ti % 2) * D:(ti % 2) * D + D],
                        in_=x_in[seq * T + ti * 128: seq * T + ti * 128 + 128, :]), "ld")

            @b.vector
            def _(v):
                for ti in range(NT):
                    v.wait_ge(self.sems["ld"], B["ld"] + (ti + 1) * 16)
                    xt = fa[:, (ti % 2) * D:(ti % 2) * D + D]
                    for c in range(4):
                        bni = v.bn_stats(self.bnb[:, c * 6:(c + 1) * 6],
                                         xt[:, c * 512:(c + 1) * 512])
                    self.fence(v, bni)
                    self.fence(v, v.bn_aggr(st1[:, 0:2], self.bnb[:]))
                    if rms_fold:
                        ua = v.tensor_scalar(st1[:, 2:3], st1[:, 1:2], 1.0 + EPS_RMS,
                                             EPS_RMS * EPS_LN, ALU.mult, ALU.add)
                    else:
                        ua = v.tensor_scalar_add(st1[:, 2:3], st1[:, 1:2], EPS_LN)
                    self.fence(v, ua)
                    self.inc(v.tensor_scalar_add(st1[:, 3:4], st1[:, 1:2], EPS_LN), "dv")
                    if ti > 0:
                        for hf in range(2):
                            v.wait_ge(self.sems["mm"], B["mm"] + 2 * (ti - 1) + hf + 1)
                            self.inc(v.tensor_copy(
                                self.big[:].rearrange("p (k t) -> p k t", t=T)
                                    [:, hf * 8:(hf + 1) * 8, (ti - 1) * 128:ti * 128],
                                self.pbf(6 + hf).rearrange("p (c q) -> p c q", q=128)
                                    [:, 0:8, :]), "s2")
                    v.wait_ge(self.sems["ev"], B["ev"] + 2 * ti + 1)
                    self.inc(v.tensor_scalar(st1[:, 6:7], st1[:, 0:1], st1[:, 4:5],
                                             -1.0, ALU.mult, ALU.mult), "d2")
                    if save_rstats:
                        idx = 48 + (seq * NT + ti) * 4
                        v.tensor_scalar_add(st1[:, idx:idx + 1], st1[:, 5:6], 1.0)
                        v.tensor_scalar(st1[:, idx + 1:idx + 2], st1[:, 0:1],
                                        st1[:, 5:6], -1.0, ALU.mult, ALU.mult)
                for hf in range(2):
                    v.wait_ge(self.sems["mm"], B["mm"] + 2 * NT - 2 + hf + 1)
                    self.inc(v.tensor_copy(
                        self.big[:].rearrange("p (k t) -> p k t", t=T)
                            [:, hf * 8:(hf + 1) * 8, (NT - 1) * 128:NT * 128],
                        self.pbf(6 + hf).rearrange("p (c q) -> p c q", q=128)
                            [:, 0:8, :]), "s2")

            @b.scalar
            def _(s):
                for ti in range(NT):
                    s.wait_ge(self.sems["dv"], B["dv"] + ti + 1)
                    s.activation(st1[:, 4:5], st1[:, 2:3], AF.Ln)
                    if save_rstats:
                        s.activation(st1[:, 5:6], st1[:, 3:4], AF.Ln)
                        s.activation(st1[:, 5:6], st1[:, 5:6], AF.Exp, scale=-0.5)
                    ai = s.activation(st1[:, 4:5], st1[:, 4:5], AF.Exp, scale=-0.5)
                    self.fence(s, ai)
                    self.inc(s.activation(st1[:, 10:11], st1[:, 4:5], AF.Copy), "ev")
                    s.wait_ge(self.sems["d2"], B["d2"] + ti + 1)
                    if ti >= 2:
                        s.wait_ge(self.sems["mm"], B["mm"] + 2 * (ti - 1))
                    self.inc(s.activation(bfa[:, (ti % 2) * D:(ti % 2) * D + D],
                                          fa[:, (ti % 2) * D:(ti % 2) * D + D],
                                          AF.Identity,
                                          bias=st1[:, 6:7], scale=st1[:, 4:5]), "ev")

            @b.tensor
            def _(t):
                for ti in range(NT):
                    t.wait_ge(self.sems["ev"], B["ev"] + 2 * ti + 2)
                    for hf in range(2):
                        if 2 * ti + hf >= 2:
                            t.wait_ge(self.sems["s2"], B["s2"] + 2 * ti + hf - 1)
                        for c8 in range(8):
                            c = hf * 8 + c8
                            mmi = t.transpose(
                                self.pbf(6 + hf)[:, c8 * 128:(c8 + 1) * 128],
                                bfa[:, (ti % 2) * D + c * 128:(ti % 2) * D + (c + 1) * 128],
                                self.id_s[:])
                        self.inc(mmi, "mm")

    def emit_proj(self, wsrc, col_offs, evac_fn):
        nc = self.nc
        NT, T = self.NT, self.T
        no = len(col_offs)
        with nc.Block() as b:
            B = self.base("wt", "mm", "ev", "st")

            @b.gpsimd
            def _(gp):
                for o in range(no):
                    if o >= 2:
                        gp.wait_ge(self.sems["ev"], B["ev"] + (o - 1) * NT)
                    self.dinc(gp.dma_start(
                        out=self.bfa[:, (o % 2) * 8192:(o % 2) * 8192 + DK * 512]
                            .rearrange("p (k q) -> p k q", q=512),
                        in_=wsrc.rearrange("(k p) q -> p k q", p=128)
                            [:, :, col_offs[o]:col_offs[o] + 512]), "wt")

            @b.tensor
            def _(t):
                gi = 0
                for o in range(no):
                    t.wait_ge(self.sems["wt"], B["wt"] + (o + 1) * 16)
                    for ti in range(NT):
                        if gi >= 4:
                            t.wait_ge(self.sems["ev"], B["ev"] + gi - 3)
                        for k in range(DK):
                            mmi = t.matmul(
                                self.psf[gi % 4][:],
                                self.big[:, k * T + ti * 128:k * T + (ti + 1) * 128],
                                self.bfa[:, (o % 2) * 8192 + k * 512:
                                          (o % 2) * 8192 + (k + 1) * 512],
                                start=(k == 0), stop=(k == DK - 1))
                        self.inc(mmi, "mm")
                        gi += 1

            @b.scalar
            def _(s):
                gi = 0
                for o in range(no):
                    for ti in range(NT):
                        s.wait_ge(self.sems["mm"], B["mm"] + gi + 1)
                        evac_fn(s, o, ti, gi % 4, B, gi)
                        gi += 1

    def emit_attn_a(self, x_in, x_out):
        T, NT = self.T, self.NT
        for seq in range(NSEQ):
            self.emit_ln(x_in, seq, rms_fold=True, save_rstats=True)
            if getattr(self, "dbg_rt", False) and seq == 0:
                with self.nc.Block() as db:
                    @db.sync
                    def _(sy):
                        self.dinc(sy.dma_start(
                            out=self.dbgrt[0].rearrange("c p t -> p c t"),
                            in_=self.big[:, 0:DK * self.T]
                                .rearrange("p (c t) -> p c t", t=self.T)), "ld")
                        sy.wait_ge(self.sems["ld"], self.cnt("ld"))

            dsts = [self.q_a] * 4 + [self.k_a] * 4 + [self.v_a] * 4

            def evac_qkv(s, o, ti, bank, B, gi, seq=seq):
                col = (o % 4) * 512
                eb = 2048 + (gi % 4) * 512
                if gi >= 4:
                    s.wait_ge(self.sems["st"], B["st"] + (gi - 3) * 16)
                self.inc(s.activation(self.fa[:, eb:eb + 512], self.psf[bank][:],
                                      AF.Copy), "ev")
                s.wait_ge(self.sems["ev"], self.cnt("ev"))
                self.dinc(s.dma_start(
                    out=dsts[o][seq * T + ti * 128: seq * T + (ti + 1) * 128,
                                col:col + 512],
                    in_=self.fa[:, eb:eb + 512]), "st")

            self.emit_proj(self.wqkvT, [i * 512 for i in range(12)], evac_qkv)
            self.emit_a3a(seq)
            self.emit_a3b(seq)
            if getattr(self, "dbg_y", False):
                with self.nc.Block() as db:
                    @db.sync
                    def _(sy, seq=seq):
                        self.dinc(sy.dma_start(
                            out=self.dbgy[seq].rearrange("c p t -> p c t"),
                            in_=self.big[:, 0:DK * self.T]
                                .rearrange("p (c t) -> p c t", t=self.T)), "ld")
                        sy.wait_ge(self.sems["ld"], self.cnt("ld"))
            self.emit_a4(seq)
        self.emit_a5(x_in, x_out)

    def emit_a3a(self, seq):
        nc = self.nc
        T, NT = self.T, self.NT
        fa, st1, bfa = self.fa, self.st1, self.bfa
        with nc.Block() as b:
            B = self.base("ld", "ev", "dv", "mm", "d2", "sg")

            @b.sync
            def _(sy):
                for ti in range(NT):
                    if ti >= 2:
                        sy.wait_ge(self.sems["ev"], B["ev"] + 3 * (ti - 1))
                    base = 6144 * (ti % 2)
                    rows = slice(seq * T + ti * 128, seq * T + (ti + 1) * 128)
                    self.dinc(sy.dma_start(out=fa[:, base:base + D],
                                           in_=self.q_a[rows, :]), "ld")
                    self.dinc(sy.dma_start(out=fa[:, base + D:base + 2 * D],
                                           in_=self.k_a[rows, :]), "ld")
                    self.dinc(sy.dma_start(out=fa[:, base + 2 * D:base + 3 * D],
                                           in_=self.v_a[rows, :]), "ld")

            @b.vector
            def _(v):
                cnt = 0
                for ti in range(NT):
                    v.wait_ge(self.sems["ld"], B["ld"] + (ti + 1) * 48)
                    base = 6144 * (ti % 2)
                    for j in range(2):
                        xt = fa[:, base + j * D:base + (j + 1) * D]
                        for c in range(4):
                            bni = v.bn_stats(self.bnb[:, c * 6:(c + 1) * 6],
                                             xt[:, c * 512:(c + 1) * 512])
                        self.fence(v, bni)
                        self.fence(v, v.bn_aggr(st1[:, 0:2], self.bnb[:]))
                        self.fence(v, v.tensor_mul(st1[:, 2 + j:3 + j], st1[:, 0:1],
                                                   st1[:, 0:1]))
                        self.fence(v, v.tensor_add(st1[:, 2 + j:3 + j],
                                                   st1[:, 2 + j:3 + j], st1[:, 1:2]))
                        self.inc(v.tensor_scalar_add(st1[:, 2 + j:3 + j],
                                                     st1[:, 2 + j:3 + j], EPS_RMS), "dv")
                    if ti > 0:
                        for u in range(6):
                            v.wait_ge(self.sems["mm"], B["mm"] + (ti - 1) * 6 + u + 1)
                            if cnt >= 2:
                                v.wait_ge(self.sems["sg"], B["sg"] + (cnt - 1) * 16)
                            sb = 24576 + (cnt % 2) * 1024
                            self.fence(v, v.tensor_copy(bfa[:, sb:sb + 1024],
                                                        self.pbf(6 + u % 2)))
                            self.inc(v.tensor_copy(self.st1[:, 11:12],
                                                   self.st1[:, 11:12]), "d2")
                            cnt += 1
                for u in range(6):
                    v.wait_ge(self.sems["mm"], B["mm"] + (NT - 1) * 6 + u + 1)
                    if cnt >= 2:
                        v.wait_ge(self.sems["sg"], B["sg"] + (cnt - 1) * 16)
                    sb = 24576 + (cnt % 2) * 1024
                    self.inc(v.tensor_copy(bfa[:, sb:sb + 1024],
                                           self.pbf(6 + u % 2)), "d2")
                    cnt += 1

            @b.scalar
            def _(s):
                for ti in range(NT):
                    s.wait_ge(self.sems["dv"], B["dv"] + 2 * ti + 2)
                    for j in range(2):
                        s.activation(st1[:, 8 + j:9 + j], st1[:, 2 + j:3 + j], AF.Ln)
                        ei = s.activation(st1[:, 8 + j:9 + j], st1[:, 8 + j:9 + j],
                                          AF.Exp, scale=-0.5)
                    self.fence(s, ei)
                    if ti >= 2:
                        s.wait_ge(self.sems["mm"], B["mm"] + 6 * (ti - 1))
                    base = 6144 * (ti % 2)
                    bb = (ti % 2) * 3 * D
                    self.inc(s.activation(bfa[:, bb:bb + D], fa[:, base:base + D],
                                          AF.Copy, scale=st1[:, 8:9]), "ev")
                    self.inc(s.activation(bfa[:, bb + D:bb + 2 * D],
                                          fa[:, base + D:base + 2 * D],
                                          AF.Copy, scale=st1[:, 9:10]), "ev")
                    self.inc(s.activation(bfa[:, bb + 2 * D:bb + 3 * D],
                                          fa[:, base + 2 * D:base + 3 * D],
                                          AF.Copy), "ev")

            @b.tensor
            def _(t):
                u = 0
                for ti in range(NT):
                    for j in range(3):
                        t.wait_ge(self.sems["ev"], B["ev"] + 3 * ti + j + 1)
                        bb = (ti % 2) * 3 * D + j * D
                        for hf in range(2):
                            if u >= 2:
                                t.wait_ge(self.sems["d2"], B["d2"] + u - 1)
                            for c8 in range(8):
                                c = hf * 8 + c8
                                mmi = t.transpose(
                                    self.pbf(6 + u % 2)[:, c8 * 128:(c8 + 1) * 128],
                                    bfa[:, bb + c * 128:bb + (c + 1) * 128], self.id_s[:])
                            self.inc(mmi, "mm")
                            u += 1

            @b.gpsimd
            def _(gp):
                dsts = (self.qrs_ft, self.krs_ft, self.v_ft)
                cnt = 0
                for ti in range(NT):
                    for j in range(3):
                        for hf in range(2):
                            gp.wait_ge(self.sems["d2"], B["d2"] + cnt + 1)
                            sb = 24576 + (cnt % 2) * 1024
                            self.dinc(gp.dma_start(
                                out=dsts[j][seq, hf * 8:(hf + 1) * 8, :,
                                            ti * 128:(ti + 1) * 128]
                                    .rearrange("c p q -> p c q"),
                                in_=bfa[:, sb:sb + 1024]
                                    .rearrange("p (c q) -> p c q", q=128)), "sg")
                            cnt += 1

    def emit_a3b(self, seq):
        nc = self.nc
        T = self.T
        fa, bfa = self.fa, self.bfa
        with nc.Block() as b:
            B = self.base("ld", "ev", "dv")

            @b.sync
            def _(sy):
                for c in range(DK):
                    if c >= 2:
                        sy.wait_ge(self.sems["dv"], B["dv"] + c - 1)
                    bb = (c % 2) * 3 * T
                    self.dinc(sy.dma_start(out=bfa[:, bb:bb + T],
                                           in_=self.qrs_ft[seq, c, :, :]), "ld")
                    self.dinc(sy.dma_start(out=bfa[:, bb + T:bb + 2 * T],
                                           in_=self.krs_ft[seq, c, :, :]), "ld")
                    self.dinc(sy.dma_start(out=bfa[:, bb + 2 * T:bb + 3 * T],
                                           in_=self.v_ft[seq, c, :, :]), "ld")

            @b.vector
            def _(v):
                self.fence(v, v.memset(bfa[:, 28672:28672 + T], 1.0))
                ones = bfa[:, 28672:28672 + T]
                for c in range(DK):
                    bb = (c % 2) * 3 * T
                    v.wait_ge(self.sems["ev"], B["ev"] + 2 * c + 2)
                    v.tensor_mul(fa[:, 4096:4096 + T], fa[:, 0:T],
                                 bfa[:, bb + 2 * T:bb + 3 * T])
                    v.tensor_tensor_scan(fa[:, 6144:6144 + T], ones, fa[:, 4096:4096 + T],
                                         0.0, ALU.mult, ALU.add)
                    self.fence(v, v.tensor_tensor_scan(fa[:, 8192:8192 + T], ones,
                                                       fa[:, 0:T], 0.0, ALU.mult, ALU.add))
                    v.tensor_scalar_add(fa[:, 8192:8192 + T], fa[:, 8192:8192 + T], 1e-6)
                    self.fence(v, v.reciprocal(fa[:, 10240:10240 + T],
                                               fa[:, 8192:8192 + T]))
                    v.tensor_mul(fa[:, 6144:6144 + T], fa[:, 6144:6144 + T],
                                 fa[:, 10240:10240 + T])
                    self.inc(v.tensor_mul(self.big[:, c * T:(c + 1) * T],
                                          fa[:, 6144:6144 + T], fa[:, 2048:2048 + T]), "dv")

            @b.scalar
            def _(s):
                for c in range(DK):
                    s.wait_ge(self.sems["ld"], B["ld"] + (c + 1) * 48)
                    if c >= 1:
                        s.wait_ge(self.sems["dv"], B["dv"] + c)
                    bb = (c % 2) * 3 * T
                    self.inc(s.activation(fa[:, 0:T], bfa[:, bb + T:bb + 2 * T],
                                          AF.Exp), "ev")
                    self.inc(s.activation(fa[:, 2048:2048 + T], bfa[:, bb:bb + T],
                                          AF.Sigmoid), "ev")

    def emit_a4(self, seq):
        nc = self.nc
        T, NT = self.T, self.NT
        fa, bfa = self.fa, self.bfa
        bfa2 = self.fa[:].bitcast(BF16)
        with nc.Block() as b:
            B = self.base("wt", "mm", "m2", "ev", "dv", "d2", "st")

            @b.gpsimd
            def _(gp):
                for o in range(16):
                    for part in range(2):
                        j = 2 * o + part
                        if j >= 4:
                            gp.wait_ge(self.sems["mm"], B["mm"] + (o - 1) * NT)
                        col = o * 512 + part * DFF
                        self.dinc(gp.dma_start(
                            out=bfa[:, (j % 4) * 8192:(j % 4) * 8192 + DK * 512]
                                .rearrange("p (k q) -> p k q", q=512),
                            in_=self.wswT.rearrange("(k p) q -> p k q", p=128)
                                [:, :, col:col + 512]), "wt")

            @b.sync
            def _(sy):
                for o in range(16):
                    for ti in range(NT):
                        cnt_g = o * NT + ti
                        sy.wait_ge(self.sems["d2"], B["d2"] + cnt_g + 1)
                        sb = 3072 + (cnt_g % 2) * 512
                        self.dinc(sy.dma_start(
                            out=self.h_a[seq, o * 4:(o + 1) * 4, :,
                                         ti * 128:(ti + 1) * 128]
                                .rearrange("c p q -> p c q"),
                            in_=bfa2[:, sb:sb + 512]
                                .rearrange("p (c q) -> p c q", q=128)), "st")

            @b.tensor
            def _(t):
                for o in range(16):
                    t.wait_ge(self.sems["wt"], B["wt"] + (2 * o + 2) * 16)
                    for ti in range(NT):
                        gi = o * NT + ti
                        if gi >= 2:
                            t.wait_ge(self.sems["ev"], B["ev"] + gi - 1)
                            t.wait_ge(self.sems["dv"], B["dv"] + gi - 1)
                        for part in range(2):
                            bank = part * 2 + gi % 2
                            bi = (2 * o + part) % 4
                            for k in range(DK):
                                mmi = t.matmul(
                                    self.psf[bank][:],
                                    self.big[:, k * T + ti * 128:k * T + (ti + 1) * 128],
                                    bfa[:, bi * 8192 + k * 512:bi * 8192 + (k + 1) * 512],
                                    start=(k == 0), stop=(k == DK - 1))
                        self.inc(mmi, "mm")
                        if gi >= 2:
                            gp_ = gi - 2
                            t.wait_ge(self.sems["dv"], B["dv"] + gp_ + 1)
                            if gp_ >= 2:
                                t.wait_ge(self.sems["d2"], B["d2"] + gp_ - 1)
                            hb = 2048 + (gp_ % 2) * 512
                            for q in range(4):
                                mmi = t.transpose(
                                    self.pbf(6 + gp_ % 2)[:, q * 128:(q + 1) * 128],
                                    bfa2[:, hb + q * 128:hb + (q + 1) * 128], self.id_s[:])
                            self.inc(mmi, "m2")
                for gp_ in range(16 * NT - 2, 16 * NT):
                    t.wait_ge(self.sems["dv"], B["dv"] + gp_ + 1)
                    t.wait_ge(self.sems["d2"], B["d2"] + gp_ - 1)
                    hb = 2048 + (gp_ % 2) * 512
                    for q in range(4):
                        mmi = t.transpose(self.pbf(6 + gp_ % 2)[:, q * 128:(q + 1) * 128],
                                          bfa2[:, hb + q * 128:hb + (q + 1) * 128],
                                          self.id_s[:])
                    self.inc(mmi, "m2")

            @b.scalar
            def _(s):
                for gi in range(16 * NT):
                    s.wait_ge(self.sems["mm"], B["mm"] + gi + 1)
                    if gi >= 2:
                        s.wait_ge(self.sems["dv"], B["dv"] + gi - 1)
                    bank_g = 2 + gi % 2
                    self.inc(s.activation(fa[:, (gi % 2) * 512:(gi % 2) * 512 + 512],
                                          self.psf[bank_g][:], AF.Silu), "ev")

            @b.vector
            def _(v):
                for gi in range(16 * NT):
                    v.wait_ge(self.sems["ev"], B["ev"] + gi + 1)
                    hb = 2048 + (gi % 2) * 512
                    if gi >= 2:
                        v.wait_ge(self.sems["m2"], B["m2"] + gi - 1)
                    self.inc(v.tensor_mul(bfa2[:, hb:hb + 512],
                                          fa[:, (gi % 2) * 512:(gi % 2) * 512 + 512],
                                          self.psf[gi % 2][:]), "dv")
                    if gi >= 2:
                        gp_ = gi - 2
                        v.wait_ge(self.sems["m2"], B["m2"] + gp_ + 1)
                        if gp_ >= 2:
                            v.wait_ge(self.sems["st"], B["st"] + (gp_ - 1) * 16)
                        sb = 3072 + (gp_ % 2) * 512
                        self.inc(v.tensor_copy(bfa2[:, sb:sb + 512],
                                               self.pbf(6 + gp_ % 2)[:, 0:512]), "d2")
                for gp_ in range(16 * NT - 2, 16 * NT):
                    v.wait_ge(self.sems["m2"], B["m2"] + gp_ + 1)
                    v.wait_ge(self.sems["st"], B["st"] + (gp_ - 1) * 16)
                    sb = 3072 + (gp_ % 2) * 512
                    self.inc(v.tensor_copy(bfa2[:, sb:sb + 512],
                                           self.pbf(6 + gp_ % 2)[:, 0:512]), "d2")

    def emit_a5(self, x_in, x_out):
        nc = self.nc
        T, NT = self.T, self.NT
        fa, bfa = self.fa, self.bfa
        for jq in range(4):
            with nc.Block() as b:
                B = self.base("wt", "ld", "mm", "ev", "dv", "st")

                @b.gpsimd
                def _(gp):
                    for hf in range(2):
                        self.dinc(gp.dma_start(
                            out=self.big[:, hf * 16384:(hf + 1) * 16384]
                                .rearrange("p (k q) -> p k q", q=512),
                            in_=self.woutT.rearrange("(k p) q -> p k q", p=128)
                                [:, hf * 32:(hf + 1) * 32, jq * 512:(jq + 1) * 512]),
                            "wt")

                @b.sync
                def _(sy):
                    for gi in range(NSEQ * NT):
                        seq, ti = divmod(gi, NT)
                        if gi >= 2:
                            sy.wait_ge(self.sems["mm"], B["mm"] + gi - 1)
                            sy.wait_ge(self.sems["ev"], B["ev"] + gi - 1)
                        rows = slice(seq * T + ti * 128, seq * T + (ti + 1) * 128)
                        self.dinc(sy.dma_start(
                            out=bfa[:, (gi % 2) * 8192:(gi % 2) * 8192 + KO * 128]
                                .rearrange("p (k q) -> p k q", q=128),
                            in_=self.h_a[seq, :, :, ti * 128:(ti + 1) * 128]
                                .rearrange("c p q -> p c q")), "ld")
                        self.dinc(sy.dma_start(
                            out=fa[:, (gi % 2) * 512:(gi % 2) * 512 + 512],
                            in_=x_in[rows, jq * 512:(jq + 1) * 512]), "ld")

                @b.tensor
                def _(t):
                    for gi in range(NSEQ * NT):
                        t.wait_ge(self.sems["ld"], B["ld"] + (gi + 1) * 32)
                        if gi >= 2:
                            t.wait_ge(self.sems["ev"], B["ev"] + gi - 1)
                        for k in range(KO):
                            if gi == 0 and k == 0:
                                t.wait_ge(self.sems["wt"], B["wt"] + 16)
                            if gi == 0 and k == 32:
                                t.wait_ge(self.sems["wt"], B["wt"] + 32)
                            mmi = t.matmul(
                                self.psf[gi % 2][:],
                                bfa[:, (gi % 2) * 8192 + k * 128:
                                    (gi % 2) * 8192 + (k + 1) * 128],
                                self.big[:, k * 512:(k + 1) * 512],
                                start=(k == 0), stop=(k == KO - 1))
                        self.inc(mmi, "mm")

                @b.scalar
                def _(s):
                    for gi in range(NSEQ * NT):
                        seq, ti = divmod(gi, NT)
                        s.wait_ge(self.sems["mm"], B["mm"] + gi + 1)
                        idx = 48 + (seq * NT + ti) * 4
                        self.inc(s.activation(
                            fa[:, 1024 + (gi % 2) * 512:1024 + (gi % 2) * 512 + 512],
                            fa[:, (gi % 2) * 512:(gi % 2) * 512 + 512], AF.Identity,
                            bias=self.st1[:, idx + 1:idx + 2],
                            scale=self.st1[:, idx:idx + 1]), "ev")
                        s.wait_ge(self.sems["dv"], B["dv"] + gi + 1)
                        self.dinc(s.dma_start(
                            out=x_out[rows if False else slice(seq * T + ti * 128,
                                                              seq * T + (ti + 1) * 128),
                                      jq * 512:(jq + 1) * 512],
                            in_=fa[:, 2048 + (gi % 2) * 512:2048 + (gi % 2) * 512 + 512]),
                            "st")

                @b.vector
                def _(v):
                    for gi in range(NSEQ * NT):
                        v.wait_ge(self.sems["ev"], B["ev"] + gi + 1)
                        if gi >= 2:
                            v.wait_ge(self.sems["st"], B["st"] + (gi - 1) * 16)
                        self.fence(v, v.tensor_add(
                            fa[:, 2048 + (gi % 2) * 512:2048 + (gi % 2) * 512 + 512],
                            fa[:, 1024 + (gi % 2) * 512:1024 + (gi % 2) * 512 + 512],
                            self.psf[gi % 2][:]))
                        self.inc(v.tensor_copy(self.st1[:, 11:12], self.st1[:, 11:12]),
                                 "dv")

    def emit_ffn(self, x_in, x_out):
        for seq in range(NSEQ):
            self.emit_ln(x_in, seq, rms_fold=False, save_rstats=False)
            self.emit_f2(seq)
        for seq in range(NSEQ):
            self.emit_f3(x_in, x_out, seq)

    def emit_f2(self, seq):
        nc = self.nc
        T, NT, TC = self.T, self.NT, self.TC
        fa, bfa = self.fa, self.bfa
        with nc.Block() as b:
            B = self.base("wt", "mm", "ev", "dv", "sg")

            @b.sync
            def _(sy):
                for hc in range(HC):
                    if hc >= 2:
                        sy.wait_ge(self.sems["mm"], B["mm"] + (hc - 1) * TC)
                    self.dinc(sy.dma_start(
                        out=bfa[:, (hc % 2) * 2048:(hc % 2) * 2048 + DK * 128]
                            .rearrange("p (k q) -> p k q", q=128),
                        in_=self.wg_bf[hc].rearrange("k p q -> p k q")), "ld")
                    self.dinc(sy.dma_start(
                        out=bfa[:, 4096 + (hc % 2) * 2048:4096 + (hc % 2) * 2048 + DK * 128]
                            .rearrange("p (k q) -> p k q", q=128),
                        in_=self.wu_bf[hc].rearrange("k p q -> p k q")), "ld")

            @b.tensor
            def _(t):
                ld0 = self.cnt("ld") - HC * 32
                for hc in range(HC):
                    t.wait_ge(self.sems["ld"], ld0 + (hc + 1) * 32)
                    for tc in range(TC):
                        gi = hc * TC + tc
                        if gi >= 2:
                            t.wait_ge(self.sems["ev"], B["ev"] + gi - 1)
                            t.wait_ge(self.sems["dv"], B["dv"] + gi - 1)
                        for part in range(2):
                            bank = part * 2 + gi % 2
                            wb = (hc % 2) * 2048 + part * 4096
                            for k in range(DK):
                                mmi = t.matmul(
                                    self.psf[bank][:],
                                    bfa[:, wb + k * 128:wb + (k + 1) * 128],
                                    self.big[:, k * T + tc * 512:k * T + (tc + 1) * 512],
                                    start=(k == 0), stop=(k == DK - 1))
                        self.inc(mmi, "mm")

            @b.scalar
            def _(s):
                for hc in range(HC):
                    for tc in range(TC):
                        gi = hc * TC + tc
                        s.wait_ge(self.sems["mm"], B["mm"] + gi + 1)
                        bank_g = gi % 2
                        self.inc(s.activation(fa[:, (gi % 2) * 512:(gi % 2) * 512 + 512],
                                              self.psf[bank_g][:], AF.Silu,
                                              scale=self.sg_s[:, hc:hc + 1]), "ev")

            @b.vector
            def _(v):
                for hc in range(HC):
                    for tc in range(TC):
                        gi = hc * TC + tc
                        v.wait_ge(self.sems["ev"], B["ev"] + gi + 1)
                        if gi >= 2:
                            v.wait_ge(self.sems["sg"], B["sg"] + (gi - 1) * 16)
                        bank_u = 2 + gi % 2
                        self.fence(v, v.scalar_tensor_tensor(
                            bfa[:, 8192 + (gi % 2) * 512:8192 + (gi % 2) * 512 + 512],
                            self.psf[bank_u][:], self.su_s[:, hc:hc + 1],
                            fa[:, (gi % 2) * 512:(gi % 2) * 512 + 512],
                            ALU.mult, ALU.mult))
                        self.inc(v.tensor_copy(self.st1[:, 11:12], self.st1[:, 11:12]),
                                 "dv")

            @b.gpsimd
            def _(gp):
                for hc in range(HC):
                    for tc in range(TC):
                        gi = hc * TC + tc
                        gp.wait_ge(self.sems["dv"], B["dv"] + gi + 1)
                        self.dinc(gp.dma_start(
                            out=self.h_f[seq, :, hc, tc * 512:(tc + 1) * 512],
                            in_=bfa[:, 8192 + (gi % 2) * 512:8192 + (gi % 2) * 512 + 512]),
                            "sg")

    def emit_f3(self, x_in, x_out, seq):
        nc = self.nc
        T, NT = self.T, self.NT
        fa, bfa = self.fa, self.bfa
        NG = NT // 2
        with nc.Block() as b:
            B = self.base("wt", "ld", "mm", "ev", "dv", "st")

            @b.gpsimd
            def _(gp):
                for g in range(NG):
                    for k in range(HC):
                        j = g * HC + k
                        if j >= 2:
                            gp.wait_ge(self.sems["mm"], B["mm"] + j - 1)
                        self.dinc(gp.dma_start(
                            out=bfa[:, (j % 2) * 2048:(j % 2) * 2048 + D],
                            in_=self.wd_bf[k]), "wt")

            @b.sync
            def _(sy):
                for g in range(NG):
                    for u in range(2):
                        ti = g * 2 + u
                        if ti >= 2:
                            sy.wait_ge(self.sems["dv"], B["dv"] + (ti - 1) * 4)
                        rows = slice(seq * T + ti * 128, seq * T + (ti + 1) * 128)
                        self.dinc(sy.dma_start(
                            out=bfa[:, 8192 + (ti % 2) * 5632:8192 + (ti % 2) * 5632 + HC * 128]
                                .rearrange("p (k q) -> p k q", q=128),
                            in_=self.h_f[seq, :, :, ti * 128:(ti + 1) * 128]
                                .rearrange("p c q -> p c q")), "ld")
                        self.dinc(sy.dma_start(out=fa[:, (ti % 2) * D:(ti % 2) * D + D],
                                               in_=x_in[rows, :]), "ld")

            @b.tensor
            def _(t):
                for g in range(NG):
                    t.wait_ge(self.sems["ld"], B["ld"] + (g * 2 + 2) * 32)
                    if g >= 1:
                        t.wait_ge(self.sems["dv"], B["dv"] + g * 8)
                    for k in range(HC):
                        j = g * HC + k
                        t.wait_ge(self.sems["wt"], B["wt"] + (j + 1) * 16)
                        for u in range(2):
                            ti = g * 2 + u
                            for o2 in range(4):
                                mmi = t.matmul(
                                    self.psf[u * 4 + o2][:],
                                    bfa[:, 8192 + (ti % 2) * 5632 + k * 128:
                                        8192 + (ti % 2) * 5632 + (k + 1) * 128],
                                    bfa[:, (j % 2) * 2048 + o2 * 512:
                                        (j % 2) * 2048 + (o2 + 1) * 512],
                                    start=(k == 0), stop=(k == HC - 1))
                        self.inc(mmi, "mm")

            @b.vector
            def _(v):
                for g in range(NG):
                    v.wait_ge(self.sems["mm"], B["mm"] + (g + 1) * HC)
                    for u in range(2):
                        ti = g * 2 + u
                        if ti >= 2:
                            v.wait_ge(self.sems["st"], B["st"] + (ti - 1) * 16)
                        for o2 in range(4):
                            ai = v.tensor_add(
                                fa[:, 4096 + (ti % 2) * D + o2 * 512:
                                   4096 + (ti % 2) * D + (o2 + 1) * 512],
                                fa[:, (ti % 2) * D + o2 * 512:(ti % 2) * D + (o2 + 1) * 512],
                                self.psf[u * 4 + o2][:])
                        self.fence(v, ai)
                        self.inc(v.tensor_copy(self.st1[:, 11:12], self.st1[:, 11:12]),
                                 "dv", 4)

            @b.scalar
            def _(s):
                for g in range(NG):
                    for u in range(2):
                        ti = g * 2 + u
                        s.wait_ge(self.sems["dv"], B["dv"] + ti * 4 + 4)
                        rows = slice(seq * T + ti * 128, seq * T + (ti + 1) * 128)
                        self.dinc(s.dma_start(
                            out=x_out[rows, :],
                            in_=fa[:, 4096 + (ti % 2) * D:4096 + (ti % 2) * D + D]), "st")

    def emit_attn_b(self, x_in, x_out):
        T, NT = self.T, self.NT
        for seq in range(NSEQ):
            self.emit_ln(x_in, seq, rms_fold=False, save_rstats=False)
            self.emit_b2(seq)
            self.emit_b3(seq)
            for h in range(H):
                self.emit_b4(seq, h)
        self.emit_b5(x_in, x_out)

    def emit_b2(self, seq):
        nc = self.nc
        T, TC = self.T, self.TC
        fa = self.fa
        spans = [6144, 8192, 10240, 12288]
        with nc.Block() as b:
            B = self.base("mm", "ev", "e2", "dv")

            @b.tensor
            def _(t):
                gi = 0
                for g in range(4):
                    for tc in range(TC):
                        if gi >= 2:
                            t.wait_ge(self.sems["ev"], B["ev"] + gi - 1)
                        for k in range(DK):
                            mmi = t.matmul(
                                self.psf[gi % 2][0:16, :],
                                self.wsA_s[:, k * 64 + g * 16:k * 64 + (g + 1) * 16],
                                self.big[:, k * T + tc * 512:k * T + (tc + 1) * 512],
                                start=(k == 0), stop=(k == DK - 1))
                        self.inc(mmi, "mm")
                        gi += 1

            @b.scalar
            def _(s):
                gi = 0
                for g in range(4):
                    for tc in range(TC):
                        s.wait_ge(self.sems["mm"], B["mm"] + gi + 1)
                        ps = self.psf[gi % 2]
                        dst = fa[0:16, spans[g] + tc * 512:spans[g] + (tc + 1) * 512]
                        if g == 0:
                            self.inc(s.activation(dst, ps[0:16, :], AF.Identity,
                                                  bias=self.bf_s[:]), "ev")
                        else:
                            self.inc(s.activation(dst, ps[0:16, :], AF.Sigmoid), "ev")
                        gi += 1
                sc = fa[0:16, 0:T]
                s.activation(sc, fa[0:16, 6144:6144 + T], AF.Abs)
                s.activation(sc, sc, AF.Exp, scale=-1.0)
                self.inc(s.activation(sc, sc, AF.Ln, bias=1.0), "e2")

            @b.vector
            def _(v):
                z = fa[0:16, 6144:6144 + T]
                sc = fa[0:16, 0:T]
                self.fence(v, v.memset(self.bfa[0:16, 28672:28672 + T], 1.0))
                v.wait_ge(self.sems["e2"], self.cnt("e2"))
                v.tensor_scalar_min(z, z, 0.0)
                v.wait_ge(self.sems["fzv"], self.cnt("fzv"))
                self.fence(v, v.tensor_sub(z, z, sc))
                self.fence(v, v.tensor_tensor_scan(sc, self.bfa[0:16, 28672:28672 + T],
                                                   z, 0.0, ALU.mult, ALU.add))
                self.inc(v.tensor_scalar_mul(z, sc, -1.0), "d2")

    def emit_b3(self, seq):
        nc = self.nc
        T, TC = self.T, self.TC
        bfa = self.bfa
        wsrcs = (self.wqT, self.wkT, self.wvT)
        dsts = (self.q_raw, self.k_raw, self.v_raw)
        with nc.Block() as b:
            B = self.base("wt", "mm", "ev", "st")

            @b.gpsimd
            def _(gp):
                for j in range(3 * H):
                    tsr, hc = divmod(j, H)
                    if j >= 2:
                        gp.wait_ge(self.sems["mm"], B["mm"] + (j - 1) * TC)
                    self.dinc(gp.dma_start(
                        out=bfa[:, (j % 2) * 2048:(j % 2) * 2048 + DK * 128]
                            .rearrange("p (k q) -> p k q", q=128),
                        in_=wsrcs[tsr].rearrange("(k p) q -> p k q", p=128)
                            [:, :, hc * 128:(hc + 1) * 128]), "wt")

            @b.tensor
            def _(t):
                for j in range(3 * H):
                    t.wait_ge(self.sems["wt"], B["wt"] + (j + 1) * 16)
                    for tc in range(TC):
                        gi = j * TC + tc
                        if gi >= 2:
                            t.wait_ge(self.sems["ev"], B["ev"] + gi - 1)
                        for k in range(DK):
                            mmi = t.matmul(
                                self.psf[gi % 2][:],
                                bfa[:, (j % 2) * 2048 + k * 128:(j % 2) * 2048 + (k + 1) * 128],
                                self.big[:, k * T + tc * 512:k * T + (tc + 1) * 512],
                                start=(k == 0), stop=(k == DK - 1))
                        self.inc(mmi, "mm")

            @b.scalar
            def _(s):
                for j in range(3 * H):
                    tsr, hc = divmod(j, H)
                    for tc in range(TC):
                        gi = j * TC + tc
                        s.wait_ge(self.sems["mm"], B["mm"] + gi + 1)
                        if gi >= 2:
                            s.wait_ge(self.sems["st"], B["st"] + (gi - 1) * 16)
                        eb = 4096 + (gi % 2) * 512
                        self.inc(s.activation(bfa[:, eb:eb + 512], self.psf[gi % 2][:],
                                              AF.Copy), "ev")
                        s.wait_ge(self.sems["ev"], self.cnt("ev"))
                        self.dinc(s.dma_start(
                            out=dsts[tsr][seq, hc, :, tc * 512:(tc + 1) * 512],
                            in_=bfa[:, eb:eb + 512]), "st")

    def emit_b4(self, seq, h):
        nc = self.nc
        T, NT, TC = self.T, self.NT, self.TC
        fa, bfa, st1 = self.fa, self.bfa, self.st1
        RQ, RK, RV = 0, 2560, 5120
        Q, K, VM, VT = 7680, 10240, 12800, 15360
        P, PTB, OB = 17920, 20480, 23040
        gates = [fa[0:16, 8192:8192 + T], fa[0:16, 10240:10240 + T],
                 fa[0:16, 12288:12288 + T]]
        ncum_row = fa[0:16, 6144:6144 + T]
        raws = [RQ, RK, RV]
        outs = [Q, K, VM]
        srcs = (self.q_raw, self.k_raw, self.v_raw)
        ohh = self.oneh_s[:, h * 128:(h + 1) * 128]

        nch = [(qt * 128 + 128 + 511) // 512 for qt in range(NT)]
        nbat = [(qt + 1 + 7) // 8 for qt in range(NT)]
        mm_qk_done = [0] * NT
        mm_pv_done = [0] * NT
        c_ = 4 * TC
        for qt in range(NT + 1):
            if qt < NT:
                c_ += nch[qt]
                mm_qk_done[qt] = c_
            if qt >= 1:
                c_ += 1
                mm_pv_done[qt - 1] = c_
        dv_done = []
        c_ = 3
        dv_start = []
        for qt in range(NT):
            dv_start.append(c_)
            c_ += nch[qt] + 2
            dv_done.append(c_)
        d2_rv = 3 * TC
        d2_pscale = []
        d2_pt_done = []
        c_ = d2_rv
        for qt in range(NT):
            c_ += 1
            d2_pscale.append(c_)
            c_ += nbat[qt]
            d2_pt_done.append(c_)
        m2_pt_done = []
        c_ = 3 * TC
        m2_pt_start = []
        for qt in range(NT):
            m2_pt_start.append(c_)
            c_ += nbat[qt]
            m2_pt_done.append(c_)

        with nc.Block() as b:
            B = self.base("ld", "mm", "m2", "ev", "e2", "dv", "d2", "sg")

            @b.sync
            def _(sy):
                for j in range(3):
                    self.dinc(sy.dma_start(out=bfa[:, raws[j] + 1:raws[j] + 1 + T],
                                           in_=srcs[j][seq, h, :, :]), "ld")

            @b.tensor
            def _(t):
                for j in range(3):
                    for c in range(TC):
                        gi = j * TC + c
                        if gi >= 2:
                            t.wait_ge(self.sems["ev"], B["ev"] + gi - 1)
                        self.inc(t.matmul(self.psf[4 + gi % 2][:], ohh,
                                          gates[j][:, c * 512:(c + 1) * 512],
                                          start=True, stop=True), "mm")
                for c in range(TC):
                    gi = 3 * TC + c
                    t.wait_ge(self.sems["ev"], B["ev"] + gi - 1)
                    self.inc(t.matmul(self.psf[4 + gi % 2][:], ohh,
                                      ncum_row[:, c * 512:(c + 1) * 512],
                                      start=True, stop=True), "mm")
                for j in range(2):
                    t.wait_ge(self.sems["dv"], B["dv"] + j + 1)
                    for tc in range(TC):
                        gi = j * TC + tc
                        if gi < 2:
                            t.wait_ge(self.sems["ev"], B["ev"] + 4 * TC)
                        else:
                            t.wait_ge(self.sems["d2"], B["d2"] + gi - 1)
                        cs = slice(outs[j] + tc * 512, outs[j] + (tc + 1) * 512)
                        self.inc(t.matmul(self.psf[4 + gi % 2][:], self.pswap_s[:],
                                          bfa[:, cs], start=True, stop=True), "m2")
                t.wait_ge(self.sems["dv"], B["dv"] + 3)
                for hf in range(TC):
                    if hf >= 2:
                        t.wait_ge(self.sems["d2"], B["d2"] + 2 * TC + hf - 1)
                    for c4 in range(4):
                        c = hf * 4 + c4
                        mmi = t.transpose(
                            self.pbf(6 + hf % 2)[:, c4 * 128:(c4 + 1) * 128],
                            bfa[:, VM + c * 128:VM + (c + 1) * 128], self.id_s[:])
                    self.inc(mmi, "m2")
                t.wait_ge(self.sems["d2"], B["d2"] + 2 * TC)
                for qq in range(NT + 1):
                    if qq < NT:
                        qt = qq
                        for c in range(nch[qt]):
                            w = min(512, (qt + 1) * 128 - c * 512)
                            if qt >= 1 and nch[qt - 1] > c:
                                t.wait_ge(self.sems["dv"],
                                          B["dv"] + dv_start[qt - 1] + c + 1)
                            self.inc(t.matmul(self.psf[c][:, 0:w],
                                              bfa[:, Q + qt * 128:Q + (qt + 1) * 128],
                                              bfa[:, K + c * 512:K + c * 512 + w],
                                              start=True, stop=True), "mm")
                    if qq >= 1:
                        qt = qq - 1
                        for r in range(nbat[qt]):
                            t.wait_ge(self.sems["d2"], B["d2"] + d2_pscale[qt])
                            if r > 0 or qt > 0:
                                prev = (d2_pscale[qt] + r) if r > 0 else d2_pt_done[qt - 1]
                                t.wait_ge(self.sems["d2"], B["d2"] + prev)
                            for u in range(min(8, qt + 1 - r * 8)):
                                c = r * 8 + u
                                mmi = t.transpose(self.pbf(7)[:, u * 128:(u + 1) * 128],
                                                  bfa[:, P + c * 128:P + (c + 1) * 128],
                                                  self.id_s[:])
                            self.inc(mmi, "m2")
                        t.wait_ge(self.sems["d2"], B["d2"] + d2_pt_done[qt])
                        if qt >= 1:
                            t.wait_ge(self.sems["e2"], B["e2"] + 2 * qt)
                        for c in range(qt + 1):
                            mmi = t.matmul(self.psf[5][:, 0:128],
                                           bfa[:, VT + c * 128:VT + (c + 1) * 128],
                                           bfa[:, PTB + c * 128:PTB + (c + 1) * 128],
                                           start=(c == 0), stop=(c == qt))
                        self.inc(mmi, "mm")

            @b.scalar
            def _(s):
                for j in range(3):
                    for c in range(TC):
                        gi = j * TC + c
                        s.wait_ge(self.sems["mm"], B["mm"] + gi + 1)
                        if j > 0 and c == 0:
                            s.wait_ge(self.sems["dv"], B["dv"] + j)
                        self.inc(s.activation(fa[:, c * 512:(c + 1) * 512],
                                              self.psf[4 + gi % 2][:], AF.Copy), "ev")
                for c in range(TC):
                    gi = 3 * TC + c
                    s.wait_ge(self.sems["mm"], B["mm"] + gi + 1)
                    self.inc(s.activation(fa[:, 2048 + c * 512:2048 + (c + 1) * 512],
                                          self.psf[4 + gi % 2][:], AF.Copy), "ev")
                for qt in range(NT):
                    ln_ = (qt + 1) * 128
                    s.wait_ge(self.sems["dv"], B["dv"] + dv_done[qt])
                    if qt == 1 and getattr(self, "dbg_flash", False) and seq == 0 and h == 0:
                        s.wait_ge(self.sems["sg"], B["sg"] + 6 * 16)
                    if qt >= 1:
                        s.wait_ge(self.sems["m2"], B["m2"] + m2_pt_done[qt - 1])
                    self.inc(s.activation(bfa[:, P:P + ln_], fa[:, 4096:4096 + ln_],
                                          AF.Exp, bias=st1[:, 101:102],
                                          accum_out=st1[:, 103:104]), "e2")
                    s.wait_ge(self.sems["mm"], B["mm"] + mm_pv_done[qt])
                    oi = s.activation(bfa[:, OB + qt * 128:OB + ln_],
                                      self.psf[5][:, 0:128], AF.Copy)
                    self.inc(oi, "e2")
                    if qt == NT - 1:
                        s.wait_ge(self.sems["e2"], self.cnt("e2"))
                        self.inc(s.activation(self.st1[:, 11:12], self.st1[:, 11:12],
                                              AF.Copy), "e2")

            @b.vector
            def _(v):
                if h == 0:
                    for j in range(3):
                        mi = v.memset(bfa[:, raws[j]:raws[j] + 1], 0.0)
                    self.fence(v, mi)
                for j in range(3):
                    v.wait_ge(self.sems["ld"], B["ld"] + (j + 1) * 16)
                    v.wait_ge(self.sems["ev"], B["ev"] + TC * (j + 1))
                    cur = bfa[:, raws[j] + 1:raws[j] + 1 + T]
                    sh = bfa[:, raws[j]:raws[j] + T]
                    v.tensor_sub(fa[:, 4096:4096 + T], sh, cur)
                    v.tensor_mul(fa[:, 4096:4096 + T], fa[:, 4096:4096 + T], fa[:, 0:T])
                    self.inc(v.tensor_add(bfa[:, outs[j]:outs[j] + T],
                                          fa[:, 4096:4096 + T], cur), "dv")
                for j in range(2):
                    dst = outs[j]
                    for tc in range(TC):
                        gi = j * TC + tc
                        v.wait_ge(self.sems["m2"], B["m2"] + gi + 1)
                        cs = slice(tc * 512, (tc + 1) * 512)
                        v.tensor_mul(fa[:, 4096:4096 + 512],
                                     bfa[:, dst + tc * 512:dst + (tc + 1) * 512],
                                     self.rc1_s[:, cs])
                        self.fence(v, v.tensor_mul(fa[:, 4608:4608 + 512],
                                                   self.psf[4 + gi % 2][:],
                                                   self.rc2_s[:, cs]))
                        self.inc(v.tensor_add(bfa[:, dst + tc * 512:dst + (tc + 1) * 512],
                                              fa[:, 4096:4096 + 512],
                                              fa[:, 4608:4608 + 512]), "d2")
                for hf in range(TC):
                    v.wait_ge(self.sems["m2"], B["m2"] + 2 * TC + hf + 1)
                    self.inc(v.tensor_copy(bfa[:, VT + hf * 512:VT + (hf + 1) * 512],
                                           self.pbf(6 + hf % 2)[:, 0:512]), "d2")
                v.wait_ge(self.sems["ev"], B["ev"] + 4 * TC)
                for qt in range(NT):
                    ln_ = (qt + 1) * 128
                    if qt >= 1:
                        v.wait_ge(self.sems["e2"], B["e2"] + 2 * qt - 1)
                    for c in range(nch[qt]):
                        w = min(512, ln_ - c * 512)
                        v.wait_ge(self.sems["mm"], B["mm"] + mm_qk_done[qt] - nch[qt] + c + 1)
                        self.inc(v.scalar_tensor_tensor(
                            fa[:, 4096 + c * 512:4096 + c * 512 + w],
                            self.psf[c][:, 0:w], ISQ,
                            fa[:, 2048 + c * 512:2048 + c * 512 + w],
                            ALU.mult, ALU.add), "dv")
                    v.wait_ge(self.sems["dv"], self.cnt("dv"))
                    self.inc(v.tensor_add(fa[:, 4096 + ln_ - 128:4096 + ln_],
                                          fa[:, 4096 + ln_ - 128:4096 + ln_],
                                          self.cm_s[:]), "dv")
                    v.wait_ge(self.sems["dv"], self.cnt("dv"))
                    self.fence(v, v.reduce_max(st1[:, 100:101], fa[:, 4096:4096 + ln_],
                                               AX.X))
                    self.inc(v.tensor_scalar_mul(st1[:, 101:102], st1[:, 100:101],
                                                 -1.0), "dv")
                    v.wait_ge(self.sems["e2"], B["e2"] + 2 * qt + 1)
                    self.fence(v, v.reciprocal(st1[:, 102:103], st1[:, 103:104]))
                    self.inc(v.tensor_scalar_mul(bfa[:, P:P + ln_], bfa[:, P:P + ln_],
                                                 st1[:, 102:103]), "d2")
                    for r in range(nbat[qt]):
                        v.wait_ge(self.sems["m2"], B["m2"] + m2_pt_start[qt] + r + 1)
                        if qt >= 1 and r == 0:
                            v.wait_ge(self.sems["mm"], B["mm"] + mm_pv_done[qt - 1])
                        w = min(8, qt + 1 - r * 8) * 128
                        self.inc(v.tensor_copy(
                            bfa[:, PTB + r * 1024:PTB + r * 1024 + w],
                            self.pbf(7)[:, 0:w]), "d2")

            @b.gpsimd
            def _(gp):
                if getattr(self, "dbg_flash", False) and seq == 0 and h == 0:
                    gp.wait_ge(self.sems["d2"], B["d2"] + d2_pscale[0])
                    self.dinc(gp.dma_start(out=self.dbgP, in_=bfa[:, P:P + 2048]), "sg")
                    self.dinc(gp.dma_start(out=self.dbgS, in_=fa[:, 4096:6144]), "sg")
                    self.dinc(gp.dma_start(out=self.dbgB, in_=fa[:, 2048:4096]), "sg")
                    self.dinc(gp.dma_start(out=self.dbgMLB, in_=st1[:, 100:108]), "sg")
                    self.dinc(gp.dma_start(out=self.dbgQK[0], in_=bfa[:, Q:Q + T]), "sg")
                    self.dinc(gp.dma_start(out=self.dbgQK[1], in_=bfa[:, K:K + T]), "sg")
                gp.wait_ge(self.sems["e2"], B["e2"] + 2 * NT + 1)
                self.dinc(gp.dma_start(out=self.o_ft[seq, h, :, :],
                                       in_=bfa[:, OB:OB + T]), "sg")
                gp.wait_ge(self.sems["sg"], self.cnt("sg"))

    def emit_b5(self, x_in, x_out):
        nc = self.nc
        T, NT = self.T, self.NT
        fa, bfa = self.fa, self.bfa
        with nc.Block() as b:
            B = self.base("wt", "ld", "mm", "dv", "sg")

            @b.gpsimd
            def _(gp):
                self.dinc(gp.dma_start(
                    out=self.big[:, 0:DK * 2048].rearrange("p (k q) -> p k q", q=2048),
                    in_=self.woT.rearrange("(k p) q -> p k q", p=128)), "wt")
                for gi in range(NSEQ * NT):
                    seq, ti = divmod(gi, NT)
                    gp.wait_ge(self.sems["dv"], B["dv"] + 4 * gi + 4)
                    rows = slice(seq * T + ti * 128, seq * T + (ti + 1) * 128)
                    self.dinc(gp.dma_start(
                        out=x_out[rows, :],
                        in_=fa[:, 4096 + (gi % 2) * D:4096 + (gi % 2) * D + D]), "sg")

            @b.sync
            def _(sy):
                for gi in range(NSEQ * NT):
                    seq, ti = divmod(gi, NT)
                    if gi >= 2:
                        sy.wait_ge(self.sems["mm"], B["mm"] + (gi - 1) * 4)
                        sy.wait_ge(self.sems["dv"], B["dv"] + (gi - 1) * 4)
                    rows = slice(seq * T + ti * 128, seq * T + (ti + 1) * 128)
                    self.dinc(sy.dma_start(
                        out=bfa[:, (gi % 2) * 2048:(gi % 2) * 2048 + DK * 128]
                            .rearrange("p (k q) -> p k q", q=128),
                        in_=self.o_ft[seq, :, :, ti * 128:(ti + 1) * 128]
                            .rearrange("c p q -> p c q")), "ld")
                    self.dinc(sy.dma_start(out=fa[:, (gi % 2) * D:(gi % 2) * D + D],
                                           in_=x_in[rows, :]), "ld")

            @b.tensor
            def _(t):
                t.wait_ge(self.sems["wt"], B["wt"] + 16)
                for gi in range(NSEQ * NT):
                    t.wait_ge(self.sems["ld"], B["ld"] + (gi + 1) * 32)
                    for j in range(4):
                        if 4 * gi + j >= 8:
                            t.wait_ge(self.sems["dv"], B["dv"] + 4 * gi + j - 7)
                        for k in range(DK):
                            mmi = t.matmul(
                                self.psf[(4 * gi + j) % 8][:],
                                bfa[:, (gi % 2) * 2048 + k * 128:(gi % 2) * 2048 + (k + 1) * 128],
                                self.big[:, k * 2048 + j * 512:k * 2048 + (j + 1) * 512],
                                start=(k == 0), stop=(k == DK - 1))
                        self.inc(mmi, "mm")

            @b.vector
            def _(v):
                for gi in range(NSEQ * NT):
                    if gi >= 2:
                        v.wait_ge(self.sems["sg"], B["sg"] + (gi - 1) * 16)
                    for j in range(4):
                        v.wait_ge(self.sems["mm"], B["mm"] + 4 * gi + j + 1)
                        ai = v.tensor_add(
                            fa[:, 4096 + (gi % 2) * D + j * 512:
                               4096 + (gi % 2) * D + (j + 1) * 512],
                            fa[:, (gi % 2) * D + j * 512:(gi % 2) * D + (j + 1) * 512],
                            self.psf[(4 * gi + j) % 8][:])
                    self.fence(v, ai)
                    self.inc(v.tensor_copy(self.st1[:, 11:12], self.st1[:, 11:12]),
                             "dv", 4)



def _consts(T):
    c = {}
    c["ident_bf"] = np.eye(128, dtype=ml_dtypes.bfloat16)
    ps = np.zeros((128, 128), dtype=ml_dtypes.bfloat16)
    ps[64:, :64] = np.eye(64)
    ps[:64, 64:] = np.eye(64)
    c["perm_swap"] = ps
    o2 = np.zeros((2, 128), np.float32); o2[0] = 1.0
    c["ones2"] = o2
    oh = np.zeros((16, 16 * 128), np.float32)
    for h in range(16):
        oh[h, h * 128:(h + 1) * 128] = 1.0
    c["onehots"] = oh
    inv = 1.0 / ROPE_BASE ** (np.arange(0, 128, 2, dtype=np.float64) / 128.0)
    fr = np.outer(inv, np.arange(T, dtype=np.float64))
    cos = np.cos(fr).astype(np.float32)
    sin = np.sin(fr).astype(np.float32)
    c["ropeC1"] = np.concatenate([cos, cos], 0).astype(ml_dtypes.bfloat16)
    c["ropeC2"] = np.concatenate([sin, -sin], 0).astype(ml_dtypes.bfloat16)
    cm = np.zeros((128, 128), np.float32)
    cm[np.triu_indices(128, 1)] = NEG_BIG
    c["causal_m"] = cm
    return c


def _shard_inputs(inputs, T):
    x = np.ascontiguousarray(np.asarray(inputs["L0_attn_a_x"], dtype=np.float32))
    B = x.shape[0]
    per = B // NCORES

    def t(a):
        return np.ascontiguousarray(np.asarray(a, dtype=np.float32).T)

    wsA = np.zeros((D, 64), np.float32)
    wsA[:, 0:16] = np.asarray(inputs["L0_attn_b_Wf"], np.float32).T
    wsA[:, 16:32] = np.asarray(inputs["L0_attn_b_Wsq"], np.float32).T
    wsA[:, 32:48] = np.asarray(inputs["L0_attn_b_Wsk"], np.float32).T
    wsA[:, 48:64] = np.asarray(inputs["L0_attn_b_Wsv"], np.float32).T
    wsB = np.zeros((D, 64), np.float32)
    sdp = np.zeros((2, D), np.float32)
    sdp[0] = np.asarray(inputs["L0_ffn_s_down"], np.float32)
    shared = dict(
        wqkvT=t(inputs["L0_attn_a_w_qkv"]),
        wswT=t(inputs["L0_attn_a_w_swiglu"]),
        woutT=t(inputs["L0_attn_a_w_out"]),
        wqT=t(inputs["L0_attn_b_Wq"]), wkT=t(inputs["L0_attn_b_Wk"]),
        wvT=t(inputs["L0_attn_b_Wv"]), woT=t(inputs["L0_attn_b_Wo"]),
        wsA=wsA, wsB=wsB,
        wgT_i=np.ascontiguousarray(np.asarray(inputs["L0_ffn_wq_gate"], np.int32).T),
        wuT_i=np.ascontiguousarray(np.asarray(inputs["L0_ffn_wq_up"], np.int32).T),
        wdT_i=np.ascontiguousarray(np.asarray(inputs["L0_ffn_wq_down"], np.int32).T),
        s_gate_t=np.ascontiguousarray(
            np.asarray(inputs["L0_ffn_s_gate"], np.float32).reshape(HC, 128).T),
        s_up_t=np.ascontiguousarray(
            np.asarray(inputs["L0_ffn_s_up"], np.float32).reshape(HC, 128).T),
        s_down_pad=sdp,
        bf_col=np.asarray(inputs["L0_attn_b_bf"], np.float32).reshape(16, 1),
        **_consts(T))
    in_maps = []
    for ci in range(NCORES):
        m = dict(shared)
        m["x0"] = np.ascontiguousarray(
            x[ci * per:(ci + 1) * per, :T, :].reshape(per * T, D))
        in_maps.append(m)
    return in_maps


def kernel(**inputs):
    T = 2048
    mk = MK(T_seq=T, n_sub=4)
    nc = mk.build()
    in_maps = _shard_inputs(inputs, T)
    res = run_bass_kernel_spmd(nc, in_maps, core_ids=list(range(NCORES)))
    per = 16 // NCORES
    outs = [r["y"].reshape(per, T, D) for r in res.results]
    return np.concatenate(outs, axis=0).astype(np.float32)
